# version 09
# baseline (speedup 1.0000x reference)
import numpy as np
import ml_dtypes
from contextlib import ExitStack

import concourse.bass as bass
import concourse.mybir as mybir
from concourse.bass_utils import run_bass_kernel_spmd

F32 = mybir.dt.float32
BF16 = mybir.dt.bfloat16
I32 = mybir.dt.int32
AF = mybir.ActivationFunctionType
ALU = mybir.AluOpType
AX = mybir.AxisListType

D = 4096
T = 2048
I = 11008
NC = D // 128
NI = I // 128
NTT = 4
TT = T // NTT
EPS = 1e-5
GRPS = [3] * 10 + [2]
GRP_BASE = [sum(GRPS[:g]) for g in range(len(GRPS))]
IBS = 8
IB_STARTS = list(range(0, NI, IBS))


def build_nc():
    nc = bass.Bass("TRN2", target_bir_lowering=False)

    d_xT = nc.dram_tensor("xT", [D, T], F32, kind="ExternalInput").ap()
    d_wt = nc.dram_tensor("wt", [NC, 128, NC, 128], F32, kind="ExternalInput").ap()
    d_mask = nc.dram_tensor("mask", [D, D], I32, kind="ExternalInput").ap()
    d_gwt = nc.dram_tensor("gwt", [NI, 128, NC, 128], I32, kind="ExternalInput").ap()
    d_uwt = nc.dram_tensor("uwt", [NI, 128, NC, 128], I32, kind="ExternalInput").ap()
    d_dwt = nc.dram_tensor("dwt", [NI, 128, D], I32, kind="ExternalInput").ap()
    d_sc = nc.dram_tensor("scales", [1, 3], F32, kind="ExternalInput").ap()
    d_S = nc.dram_tensor("Ssel", [128, NC, 128], BF16, kind="ExternalInput").ap()
    d_P = nc.dram_tensor("Psel", [128, NC, 128], F32, kind="ExternalInput").ap()
    d_out = nc.dram_tensor("outT", [D, T], F32, kind="ExternalOutput").ap()

    d_h1pre = nc.dram_tensor("h1pre_d", [NC, 128, T], BF16, kind="Internal").ap()
    d_h1b = nc.dram_tensor("h1b_d", [NC, 128, T], BF16, kind="Internal").ap()
    d_tact = nc.dram_tensor("tact_d", [NI, 128, T], BF16, kind="Internal").ap()
    d_dwb = nc.dram_tensor("dwb_d", [NI, 128, D], BF16, kind="Internal").ap()
    d_rows = nc.dram_tensor("rows_d", [2, T], F32, kind="Internal").ap()
    d_rows2 = nc.dram_tensor("rows2_d", [2, TT], F32, kind="Internal").ap()

    es = ExitStack()
    sb = lambda n, sh, dt: es.enter_context(nc.sbuf_tensor(n, sh, dt))
    ps = lambda n, sh: es.enter_context(nc.psum_tensor(n, sh, F32))
    sems = {k: es.enter_context(nc.semaphore(k)) for k in ("qg", "qs", "qa", "pe", "ve", "se")}
    c = {k: 0 for k in sems}

    def inc(instr, name, n=1):
        c[name] += n
        instr.then_inc(sems[name], n)
        return c[name]

    def w(eng, name, val):
        if val > 0:
            eng.wait_ge(sems[name], val)

    def fence():
        for eng in (nc.tensor, nc.vector, nc.scalar, nc.gpsimd, nc.sync):
            for s in c:
                w(eng, s, c[s])

    sc_sb = sb("sc_sb", [128, 4], F32)
    ones_bf = sb("ones_bf", [128, 1], BF16)
    inc(nc.sync.dma_start(out=sc_sb[:, 0:3], in_=d_sc.broadcast_to([128, 3])), "qs", 16)
    r_ones = inc(nc.vector.memset(ones_bf[:], 1.0), "ve")

    with nc.sbuf_tensor("X_sb", [128, NC * T], BF16) as X_sb, \
         nc.sbuf_tensor("T_sb", [128, NC * 128], BF16) as T_sb:
        Xv = X_sb.ap().rearrange("p (c t) -> p c t", t=T)

        xT_r = d_xT.rearrange("(c p) t -> p c t", p=128)
        r_xload = 0
        for j in range(4):
            r_xload = inc(nc.gpsimd.dma_start(
                out=Xv[:, 8 * j:8 * (j + 1), :], in_=xT_r[:, 8 * j:8 * (j + 1), :]), "qg", 16)

        with nc.sbuf_tensor("BTsign", [128, 128], F32) as BTsign, \
             nc.sbuf_tensor("BT01", [128, 128], BF16) as BT01, \
             nc.psum_tensor("pBT", [128, 128], F32) as pBT, \
             nc.psum_tensor("pT0", [128, 128], F32) as pT0, \
             nc.psum_tensor("pT1", [128, 128], F32) as pT1:
            pT = [pT0, pT1]
            with nc.sbuf_tensor("mstage0", [128, D], I32) as m0, \
                 nc.sbuf_tensor("mstage1", [128, D], I32) as m1, \
                 nc.sbuf_tensor("mstage2", [128, D], I32) as m2, \
                 nc.sbuf_tensor("P32_sb", [128, NC * 128], F32) as P32_sb, \
                 nc.sbuf_tensor("R0", [128, 128], F32) as R0, \
                 nc.sbuf_tensor("R1", [128, 128], F32) as R1:
                mstage = [m0, m1, m2]
                R = [R0, R1]
                r_p32 = inc(nc.sync.dma_start(out=P32_sb[:], in_=d_P.rearrange("q c m -> q (c m)")), "qs", 16)
                rec_red, rec_sel = {}, {}
                for n2 in range(NC):
                    if n2 >= 3:
                        w(nc.sync, "ve", rec_red[n2 - 3])
                    r_l = inc(nc.sync.dma_start(
                        out=mstage[n2 % 3][:], in_=d_mask[n2 * 128:(n2 + 1) * 128, :]), "qs", 16)
                    w(nc.vector, "qs", r_l)
                    if n2 >= 2:
                        w(nc.vector, "pe", rec_sel[n2 - 2])
                    rec_red[n2] = inc(nc.vector.reduce_sum(
                        R[n2 % 2][:], mstage[n2 % 3].ap().rearrange("p (k w) -> p k w", w=32),
                        axis=AX.X), "ve")
                    w(nc.tensor, "ve", rec_red[n2])
                    if n2 == 0:
                        w(nc.tensor, "qs", r_p32)
                    rec_sel[n2] = inc(nc.tensor.matmul(
                        pBT[:], R[n2 % 2][:], P32_sb[:, n2 * 128:(n2 + 1) * 128],
                        start=(n2 == 0), stop=(n2 == NC - 1)), "pe")
                w(nc.scalar, "pe", rec_sel[NC - 1])
                r_sign = inc(nc.scalar.activation(BTsign[:], pBT[:], AF.Sign), "se")
                w(nc.vector, "se", r_sign)
                r_bt01 = inc(nc.vector.tensor_scalar_max(BT01[:], BTsign[:], 0.0), "ve")
            with nc.sbuf_tensor("S_sb", [128, NC * 128], BF16) as S_sb:
                w(nc.sync, "ve", r_bt01)
                r_s = inc(nc.sync.dma_start(out=S_sb[:], in_=d_S.rearrange("q c m -> q (c m)")), "qs", 16)
                rec_tcopy = {}
                for kc in range(NC):
                    w(nc.tensor, "ve", r_bt01)
                    if kc == 0:
                        w(nc.tensor, "qs", r_s)
                    if kc >= 2:
                        w(nc.tensor, "se", rec_tcopy[kc - 2])
                    r = inc(nc.tensor.matmul(pT[kc % 2][:], S_sb[:, kc * 128:(kc + 1) * 128],
                                             BT01[:], start=True, stop=True), "pe")
                    w(nc.scalar, "pe", r)
                    rec_tcopy[kc] = inc(nc.scalar.copy(T_sb[:, kc * 128:(kc + 1) * 128], pT[kc % 2][:]), "se")
                r_Tdone = rec_tcopy[NC - 1]

        T3 = T_sb.ap().rearrange("p (c nb) -> p c nb", nb=128)
        with nc.sbuf_tensor("Wraw0", [128, D], BF16) as w0, \
             nc.sbuf_tensor("Wraw1", [128, D], BF16) as w1, \
             nc.sbuf_tensor("Wm0", [128, D], BF16) as wm0, \
             nc.sbuf_tensor("Wm1", [128, D], BF16) as wm1, \
             nc.sbuf_tensor("s1t0", [128, T // 2], BF16) as s1t0, \
             nc.sbuf_tensor("s1t1", [128, T // 2], BF16) as s1t1, \
             nc.sbuf_tensor("sqt0", [128, T // 2], BF16) as sqt0, \
             nc.sbuf_tensor("sqt1", [128, T // 2], BF16) as sqt1, \
             nc.sbuf_tensor("accs", [1, T], F32) as accs, \
             nc.sbuf_tensor("accq", [1, T], F32) as accq, \
             nc.sbuf_tensor("rowtmp", [1, T], F32) as rowtmp, \
             nc.psum_tensor("Py0", [128, T // 2], F32) as Py0, \
             nc.psum_tensor("Py1", [128, T // 2], F32) as Py1, \
             nc.psum_tensor("Ps_sum", [1, T // 2], F32) as Ps_sum, \
             nc.psum_tensor("Ps_sq", [1, T // 2], F32) as Ps_sq:
            Wraw = [w0, w1]
            Wm = [wm0, wm1]
            s1t = [s1t0, s1t1]
            sqt = [sqt0, sqt1]
            Py = [Py0, Py1]
            r_acc0 = inc(nc.vector.memset(accs[:], 0.0), "ve")
            r_acc0 = inc(nc.vector.memset(accq[:], 0.0), "ve")

            rec_wm = {}
            rec_attn = {}
            rec_evac = {}
            rec_s1store = {}
            rec_sq = {}
            rec_statmm = {}
            rec_statacc = {}

            def emit_stats(ni, th):
                w(nc.tensor, "se", rec_sq[(ni, th)])
                w(nc.tensor, "ve", rec_statacc.get((ni, th - 1), rec_statacc.get((ni - 1, 1), 0)))
                w(nc.tensor, "ve", r_ones)
                for q in range(2):
                    nc.tensor.matmul(Ps_sum[0:1, q * 512:(q + 1) * 512], ones_bf[:],
                                     s1t[th][:, q * 512:(q + 1) * 512], start=True, stop=True)
                for q in range(2):
                    mm = nc.tensor.matmul(Ps_sq[0:1, q * 512:(q + 1) * 512], ones_bf[:],
                                          sqt[th][:, q * 512:(q + 1) * 512], start=True, stop=True)
                rec_statmm[(ni, th)] = inc(mm, "pe")
                w(nc.vector, "pe", rec_statmm[(ni, th)])
                nc.vector.tensor_tensor(accs[0:1, th * 1024:(th + 1) * 1024],
                                        accs[0:1, th * 1024:(th + 1) * 1024],
                                        Ps_sum[0:1, 0:1024], ALU.add)
                vv = nc.vector.tensor_tensor(accq[0:1, th * 1024:(th + 1) * 1024],
                                             accq[0:1, th * 1024:(th + 1) * 1024],
                                             Ps_sq[0:1, 0:1024], ALU.add)
                rec_statacc[(ni, th)] = inc(vv, "ve")

            for ni in range(NC):
                if ni < 2:
                    w(nc.gpsimd, "se", r_Tdone)
                if ni >= 2:
                    w(nc.gpsimd, "ve", rec_wm[ni - 2])
                r_wl = inc(nc.gpsimd.dma_start(
                    out=Wraw[ni % 2][:], in_=d_wt[ni].rearrange("p kc n -> p (kc n)")), "qg", 16)
                w(nc.vector, "qg", r_wl)
                w(nc.vector, "se", r_Tdone)
                if ni >= 2:
                    w(nc.vector, "pe", rec_attn[(ni - 2, 1)])
                tb = T3[:, :, 4 * ni:4 * ni + 4][:, :, :, None].broadcast_to([128, NC, 4, 32])
                rec_wm[ni] = inc(nc.vector.tensor_tensor(
                    Wm[ni % 2].ap().rearrange("p (c nb r) -> p c nb r", nb=4, r=32),
                    Wraw[ni % 2].ap().rearrange("p (c nb r) -> p c nb r", nb=4, r=32),
                    tb, ALU.mult), "ve")

                if ni >= 1:
                    emit_stats(ni - 1, 1)

                for th in range(2):
                    w(nc.tensor, "ve", rec_wm[ni])
                    if ni == 0 and th == 0:
                        w(nc.tensor, "qg", r_xload)
                    if ni >= 1:
                        w(nc.tensor, "ve", rec_evac[(ni - 1, th)])
                    for q in range(2):
                        for kc in range(NC):
                            mm = nc.tensor.matmul(
                                Py[th][:, q * 512:(q + 1) * 512],
                                Wm[ni % 2][:, kc * 128:(kc + 1) * 128],
                                Xv[:, kc, th * 1024 + q * 512: th * 1024 + (q + 1) * 512],
                                start=(kc == 0), stop=(kc == NC - 1))
                    rec_attn[(ni, th)] = inc(mm, "pe")

                    w(nc.vector, "pe", rec_attn[(ni, th)])
                    if ni >= 1:
                        w(nc.vector, "qs", rec_s1store[(ni - 1, th)])
                        w(nc.vector, "pe", rec_statmm[(ni - 1, th)])
                    rec_evac[(ni, th)] = inc(nc.vector.tensor_tensor(
                        s1t[th][:], Py[th][:], Xv[:, ni, th * 1024:(th + 1) * 1024],
                        ALU.add), "ve")
                    w(nc.sync, "ve", rec_evac[(ni, th)])
                    rec_s1store[(ni, th)] = inc(nc.sync.dma_start(
                        out=d_h1pre[ni][:, th * 1024:(th + 1) * 1024], in_=s1t[th][:]), "qs", 16)
                    w(nc.scalar, "ve", rec_evac[(ni, th)])
                    if ni >= 1:
                        w(nc.scalar, "pe", rec_statmm[(ni - 1, th)])
                    rec_sq[(ni, th)] = inc(nc.scalar.activation(sqt[th][:], s1t[th][:], AF.Square), "se")

                emit_stats(ni, 0)
            emit_stats(NC - 1, 1)

            last_acc = rec_statacc[(NC - 1, 1)]
            w(nc.vector, "ve", last_acc)
            nc.vector.tensor_scalar_mul(accs[:], accs[:], 1.0 / D)
            nc.vector.tensor_scalar_mul(accq[:], accq[:], 1.0 / D)
            nc.vector.tensor_tensor(rowtmp[:], accs[:], accs[:], ALU.mult)
            nc.vector.tensor_sub(accq[:], accq[:], rowtmp[:])
            r_var = inc(nc.vector.tensor_scalar_add(accq[:], accq[:], EPS), "ve")
            w(nc.scalar, "ve", r_var)
            r_std = inc(nc.scalar.activation(accq[:], accq[:], AF.Sqrt), "se")
            w(nc.vector, "se", r_std)
            nc.vector.reciprocal(accq[:], accq[:])
            r_rows = inc(nc.vector.scalar_tensor_tensor(
                accs[:], accs[:], -1.0, accq[:], ALU.mult, ALU.mult), "ve")
            w(nc.sync, "ve", r_rows)
            inc(nc.sync.dma_start(out=d_rows[0:1, :], in_=accq[:]), "qs", 16)
            inc(nc.sync.dma_start(out=d_rows[1:2, :], in_=accs[:]), "qs", 16)
    fence()

    with nc.sbuf_tensor("h1b_sb", [128, NC * T], BF16) as h1b_sb, \
         nc.sbuf_tensor("Ab1", [128, T], BF16) as Ab1, \
         nc.sbuf_tensor("Bb1", [128, T], BF16) as Bb1:
        H = h1b_sb.ap().rearrange("p (c t) -> p c t", t=T)
        rec_apply = {}
        if True:
            r_ab = inc(nc.gpsimd.dma_start(out=Ab1[:], in_=d_rows[0:1, :].broadcast_to([128, T])), "qg", 16)
            r_bb = inc(nc.gpsimd.dma_start(out=Bb1[:], in_=d_rows[1:2, :].broadcast_to([128, T])), "qg", 16)
            rec_h1load = {}
            for dc in range(NC):
                rec_h1load[dc] = inc(nc.sync.dma_start(out=H[:, dc, :], in_=d_h1pre[dc]), "qs", 16)
            for dc in range(NC):
                eng = nc.vector if dc % 2 == 0 else nc.gpsimd
                sem = "ve" if dc % 2 == 0 else "qg"
                w(eng, "qs", rec_h1load[dc])
                w(eng, "qg", r_bb)
                eng.tensor_tensor(H[:, dc, :], H[:, dc, :], Ab1[:], ALU.mult)
                rec_apply[dc] = inc(eng.tensor_tensor(H[:, dc, :], H[:, dc, :], Bb1[:], ALU.add), sem)
                w(nc.sync, sem, rec_apply[dc])
                inc(nc.sync.dma_start(out=d_h1b[dc], in_=H[:, dc, :]), "qs", 16)
        r_apply_all_v = max(rec_apply[dc] for dc in range(0, NC, 2))
        r_apply_all_g = max(rec_apply[dc] for dc in range(1, NC, 2))

        esb = ExitStack()
        sbb = lambda n, sh, dt: esb.enter_context(nc.sbuf_tensor(n, sh, dt))
        if True:
            Gs = [sbb(f"Gs{k}", [128, D], BF16) for k in range(2)]
            Us = [sbb(f"Us{k}", [128, D], BF16) for k in range(2)]
            Ds = [sbb(f"Ds{k}", [128, D], BF16) for k in range(2)]
            tg = [sbb(f"tg{k}", [128, 1024], BF16) for k in range(2)]
            tu = [sbb(f"tu{k}", [128, 1024], BF16) for k in range(2)]
            tac = [sbb(f"tac{k}", [128, 1024], BF16) for k in range(2)]
            Pg = [esb.enter_context(nc.psum_tensor(f"Pg{k}", [128, 1024], F32)) for k in range(2)]
            Pu = [esb.enter_context(nc.psum_tensor(f"Pu{k}", [128, 1024], F32)) for k in range(2)]

            rec_gl, rec_ul, rec_dl, rec_dst = {}, {}, {}, {}
            rec_gmm, rec_umm = {}, {}
            rec_silu, rec_tuc, rec_tmul, rec_tst = {}, {}, {}, {}

            for ic in range(NI):
                if ic >= 2:
                    w(nc.gpsimd, "pe", rec_gmm[(ic - 2, 1)])
                rec_gl[ic] = inc(nc.gpsimd.dma_start(
                    out=Gs[ic % 2][:], in_=d_gwt[ic].rearrange("p dc i -> p (dc i)")), "qg", 16)
                if ic >= 2:
                    w(nc.gpsimd, "pe", rec_umm[(ic - 2, 1)])
                rec_ul[ic] = inc(nc.gpsimd.dma_start(
                    out=Us[ic % 2][:], in_=d_uwt[ic].rearrange("p dc i -> p (dc i)")), "qg", 16)
                if ic >= 2:
                    w(nc.gpsimd, "qs", rec_dst[ic - 2])
                rec_dl[ic] = inc(nc.gpsimd.dma_start(out=Ds[ic % 2][:], in_=d_dwt[ic]), "qg", 16)
                w(nc.sync, "qg", rec_dl[ic])
                rec_dst[ic] = inc(nc.sync.dma_start(out=d_dwb[ic], in_=Ds[ic % 2][:]), "qs", 16)

                for th in range(2):
                    w(nc.tensor, "qg", rec_gl[ic])
                    if ic == 0 and th == 0:
                        w(nc.tensor, "ve", r_apply_all_v)
                        w(nc.tensor, "qg", r_apply_all_g)
                    if ic >= 1:
                        w(nc.tensor, "se", rec_silu[(ic - 1, th)])
                    for q in range(2):
                        for dc in range(NC):
                            mm = nc.tensor.matmul(
                                Pg[th][:, q * 512:(q + 1) * 512],
                                Gs[ic % 2][:, dc * 128:(dc + 1) * 128],
                                H[:, dc, th * 1024 + q * 512: th * 1024 + (q + 1) * 512],
                                start=(dc == 0), stop=(dc == NC - 1))
                    rec_gmm[(ic, th)] = inc(mm, "pe")
                    w(nc.tensor, "qg", rec_ul[ic])
                    if ic >= 1:
                        w(nc.tensor, "ve", rec_tuc[(ic - 1, th)])
                    for q in range(2):
                        for dc in range(NC):
                            mm = nc.tensor.matmul(
                                Pu[th][:, q * 512:(q + 1) * 512],
                                Us[ic % 2][:, dc * 128:(dc + 1) * 128],
                                H[:, dc, th * 1024 + q * 512: th * 1024 + (q + 1) * 512],
                                start=(dc == 0), stop=(dc == NC - 1))
                    rec_umm[(ic, th)] = inc(mm, "pe")

                    w(nc.scalar, "pe", rec_gmm[(ic, th)])
                    w(nc.scalar, "qs", 16)
                    if ic >= 1:
                        w(nc.scalar, "ve", rec_tmul[(ic - 1, th)])
                    rec_silu[(ic, th)] = inc(nc.scalar.activation(
                        tg[th][:], Pg[th][:], AF.Silu, scale=sc_sb.ap()[:, 0:1]), "se")
                    w(nc.vector, "pe", rec_umm[(ic, th)])
                    if ic >= 1:
                        w(nc.vector, "ve", rec_tmul[(ic - 1, th)])
                    rec_tuc[(ic, th)] = inc(nc.vector.tensor_scalar_mul(
                        tu[th][:], Pu[th][:], sc_sb.ap()[:, 1:2]), "ve")
                    w(nc.vector, "se", rec_silu[(ic, th)])
                    if ic >= 1:
                        w(nc.vector, "qs", rec_tst[(ic - 1, th)])
                    rec_tmul[(ic, th)] = inc(nc.vector.tensor_tensor(
                        tac[th][:], tg[th][:], tu[th][:], ALU.mult), "ve")
                    w(nc.sync, "ve", rec_tmul[(ic, th)])
                    rec_tst[(ic, th)] = inc(nc.sync.dma_start(
                        out=d_tact[ic][:, th * 1024:(th + 1) * 1024], in_=tac[th][:]), "qs", 16)
        esb.close()
    fence()

    if True:
        tact_sb = sb("tact_sb", [128, (NI - IBS) * TT], BF16)
        tact0_sb = sb("tact0_sb", [128, 2 * IBS * TT], BF16)
        h1c = sb("h1c", [128, NC * TT], BF16)
        s2 = sb("s2", [128, NC * TT], BF16)
        Db = [sb(f"Db{k}", [128, IBS * 3 * 128], BF16) for k in range(4)]
        sq2 = [sb(f"sq2{k}", [128, TT], BF16) for k in range(4)]
        outst = [sb(f"outst{k}", [128, TT], F32) for k in range(4)]
        A2b = sb("A2b", [128, TT], F32)
        B2b = sb("B2b", [128, TT], F32)
        r2a = sb("r2a", [1, TT], F32)
        r2b = sb("r2b", [1, TT], F32)
        r2t = sb("r2t", [1, TT], F32)
        Pd = [ps(f"Pd{k}", [128, TT]) for k in range(6)]
        Pst_sum = ps("Pst_sum", [1, TT])
        Pst_sq = ps("Pst_sq", [1, TT])
        TA = tact_sb.ap().rearrange("p (c t) -> p c t", t=TT)
        TA0 = tact0_sb.ap().rearrange("p (h c t) -> p h c t", h=2, t=TT)
        def tact_rhs(icc, tt):
            return TA0[:, tt % 2, icc, :] if icc < IBS else TA[:, icc - IBS, :]
        H1 = h1c.ap().rearrange("p (c t) -> p c t", t=TT)
        S2 = s2.ap().rearrange("p (c t) -> p c t", t=TT)

        rec_tl = {}
        rec_tl0 = {}
        rec_b0mm = {}
        rec_h1c = {}
        rec_grp = {}
        rec_s2op = {}
        rec_sq2 = {}
        rec_st2 = {}
        rec_rows2 = {}
        rec_bcast = {}
        rec_outop = {}
        rec_outst = {}
        dmm_hist = []
        sq2_hist = []
        outst_hist = []

        for tt in range(NTT):
            if tt == 0:
                rec_tl0[0] = inc(nc.scalar.dma_start(
                    out=TA0[:, 0, :, :],
                    in_=d_tact[0:IBS, :, 0:TT].transpose([1, 0, 2])), "qa", 16)
            for bi, ic0 in enumerate(IB_STARTS):
                if bi == 0:
                    continue
                kk = min(IBS, NI - ic0)
                if tt >= 1:
                    w(nc.scalar, "pe", rec_grp[(tt - 1, len(GRPS) - 1)])
                rec_tl[(tt, bi)] = inc(nc.scalar.dma_start(
                    out=TA[:, ic0 - IBS:ic0 - IBS + kk, :],
                    in_=d_tact[ic0:ic0 + kk, :, tt * TT:(tt + 1) * TT].transpose([1, 0, 2])), "qa", 16)
            if tt + 1 < NTT:
                if tt >= 1:
                    w(nc.scalar, "pe", rec_b0mm[tt - 1])
                rec_tl0[tt + 1] = inc(nc.scalar.dma_start(
                    out=TA0[:, (tt + 1) % 2, :, :],
                    in_=d_tact[0:IBS, :, (tt + 1) * TT:(tt + 2) * TT].transpose([1, 0, 2])), "qa", 16)
            for jb in range(4):
                if tt >= 1:
                    w(nc.scalar, "ve", rec_s2op[(tt - 1, NC - 1)])
                rec_h1c[(tt, jb)] = inc(nc.scalar.dma_start(
                    out=H1[:, 8 * jb:8 * (jb + 1), :],
                    in_=d_h1b[8 * jb:8 * (jb + 1), :, tt * TT:(tt + 1) * TT].transpose([1, 0, 2])), "qa", 16)

            def emit_stats2(gg):
                for j2 in range(GRPS[gg]):
                    dc2 = GRP_BASE[gg] + j2
                    w(nc.tensor, "se", rec_sq2[(tt, dc2)])
                    if tt >= 1 and dc2 == 0:
                        w(nc.tensor, "ve", rec_rows2[tt - 1])
                    nc.tensor.matmul(Pst_sum[:], ones_bf[:], S2[:, dc2, :],
                                     start=(dc2 == 0), stop=(dc2 == NC - 1),
                                     skip_group_check=True)
                    mm2 = nc.tensor.matmul(Pst_sq[:], ones_bf[:], sq2[dc2 % 4][:],
                                           start=(dc2 == 0), stop=(dc2 == NC - 1),
                                           skip_group_check=True)
                    rec_st2[(tt, dc2)] = inc(mm2, "pe")
                    sq2_hist.append(rec_st2[(tt, dc2)])

            for g, gsz in enumerate(GRPS):
                dbase = GRP_BASE[g]
                setk = g % 2
                for bi, ic0 in enumerate(IB_STARTS):
                    kk = min(IBS, NI - ic0)
                    gblk = len(dmm_hist)
                    deng, dsem = (nc.gpsimd, "qg") if gblk % 2 == 0 else (nc.sync, "qs")
                    if len(dmm_hist) >= 4:
                        w(deng, "pe", dmm_hist[-4])
                    r_dblk = inc(deng.dma_start(
                        out=Db[gblk % 4].ap().rearrange("p (a b) -> p a b", a=IBS)[:, 0:kk, 0:gsz * 128],
                        in_=d_dwb[ic0:ic0 + kk, :, dbase * 128:(dbase + gsz) * 128].transpose([1, 0, 2])), dsem, 16)
                    w(nc.tensor, dsem, r_dblk)
                    w(nc.tensor, "qa", rec_tl0[tt] if bi == 0 else rec_tl[(tt, bi)])
                    if g >= 2:
                        w(nc.tensor, "ve", rec_s2op[(tt, GRP_BASE[g - 2] + GRPS[g - 2] - 1)])
                    elif tt >= 1:
                        w(nc.tensor, "ve", rec_s2op[(tt - 1, NC - 1)])
                    Dv = Db[gblk % 4].ap().rearrange("p (a b) -> p a b", a=IBS)
                    for k in range(kk):
                        icc = ic0 + k
                        for j in range(gsz):
                            mm = nc.tensor.matmul(
                                Pd[setk * 3 + j][:],
                                Dv[:, k, j * 128:(j + 1) * 128],
                                tact_rhs(icc, tt),
                                start=(icc == 0), stop=(icc == NI - 1),
                                skip_group_check=True)
                    dmm_hist.append(inc(mm, "pe"))
                    if bi == 0 and g == len(GRPS) - 1:
                        rec_b0mm[tt] = dmm_hist[-1]
                rec_grp[(tt, g)] = dmm_hist[-1]

                if g >= 1:
                    emit_stats2(g - 1)

                for j in range(gsz):
                    dc = dbase + j
                    w(nc.vector, "pe", rec_grp[(tt, g)])
                    w(nc.vector, "qa", rec_h1c[(tt, dc // 8)])
                    rec_s2op[(tt, dc)] = inc(nc.vector.scalar_tensor_tensor(
                        S2[:, dc, :], Pd[setk * 3 + j][:], sc_sb.ap()[:, 2:3],
                        H1[:, dc, :], ALU.mult, ALU.add), "ve")
                    w(nc.scalar, "ve", rec_s2op[(tt, dc)])
                    if len(sq2_hist) >= 4:
                        w(nc.scalar, "pe", sq2_hist[-4])
                    rec_sq2[(tt, dc)] = inc(nc.scalar.activation(
                        sq2[dc % 4][:], S2[:, dc, :], AF.Square), "se")
            emit_stats2(len(GRPS) - 1)

            w(nc.vector, "pe", rec_st2[(tt, NC - 1)])
            nc.vector.tensor_scalar_mul(r2a[:], Pst_sum[0:1, :], 1.0 / D)
            nc.vector.tensor_scalar_mul(r2b[:], Pst_sq[0:1, :], 1.0 / D)
            nc.vector.tensor_tensor(r2t[:], r2a[:], r2a[:], ALU.mult)
            nc.vector.tensor_sub(r2b[:], r2b[:], r2t[:])
            r_v2 = inc(nc.vector.tensor_scalar_add(r2b[:], r2b[:], EPS), "ve")
            rec_rows2[tt] = r_v2
            w(nc.scalar, "ve", r_v2)
            r_sd2 = inc(nc.scalar.activation(r2b[:], r2b[:], AF.Sqrt), "se")
            w(nc.vector, "se", r_sd2)
            r_rw2 = inc(nc.vector.reciprocal(r2b[:], r2b[:]), "ve")
            w(nc.sync, "ve", r_rw2)
            r_st_rows = inc(nc.sync.dma_start(out=d_rows2[0:1, :], in_=r2b[:]), "qs", 16)
            r_st_rows = inc(nc.sync.dma_start(out=d_rows2[1:2, :], in_=r2a[:]), "qs", 16)
            w(nc.sync, "qs", r_st_rows)
            if tt >= 1:
                w(nc.sync, "ve", rec_outop[(tt - 1, NC - 1)])
            inc(nc.sync.dma_start(out=A2b[:], in_=d_rows2[0:1, :].broadcast_to([128, TT])), "qs", 16)
            rec_bcast[tt] = inc(nc.sync.dma_start(out=B2b[:], in_=d_rows2[1:2, :].broadcast_to([128, TT])), "qs", 16)

            for dc in range(NC):
                w(nc.vector, "qs", rec_bcast[tt])
                if len(outst_hist) >= 4:
                    w(nc.vector, "qs", outst_hist[-4])
                nc.vector.tensor_tensor(outst[dc % 4][:], S2[:, dc, :], B2b[:], ALU.subtract)
                rec_outop[(tt, dc)] = inc(nc.vector.tensor_tensor(
                    outst[dc % 4][:], outst[dc % 4][:], A2b[:], ALU.mult), "ve")
                w(nc.sync, "ve", rec_outop[(tt, dc)])
                rec_outst[(tt, dc)] = inc(nc.sync.dma_start(
                    out=d_out[dc * 128:(dc + 1) * 128, tt * TT:(tt + 1) * TT],
                    in_=outst[dc % 4][:]), "qs", 16)
                outst_hist.append(rec_outst[(tt, dc)])

        w(nc.sync, "qs", c["qs"])
        w(nc.sync, "qg", c["qg"])
        w(nc.sync, "pe", c["pe"])
        w(nc.sync, "ve", c["ve"])
        w(nc.sync, "se", c["se"])
    es.close()
    return nc


def _host_consts():
    S_host = np.zeros((128, 32, 128), dtype=ml_dtypes.bfloat16)
    for kc in range(32):
        for m in range(128):
            S_host[4 * kc + m // 32, kc, m] = 1
    P_host = np.zeros((128, 32, 128), dtype=np.float32)
    for ncc in range(32):
        for q in range(128):
            P_host[q, ncc, 4 * ncc + q // 32] = 1
    return S_host, P_host


def kernel(L0_attn_data, L0_attn_weight, L0_attn_mask, L0_ffn_gate_w, L0_ffn_gate_s,
           L0_ffn_up_w, L0_ffn_up_s, L0_ffn_down_w, L0_ffn_down_s):
    x = np.asarray(L0_attn_data, dtype=np.float32)
    W = np.asarray(L0_attn_weight, dtype=np.float32)
    mask = np.ascontiguousarray(np.asarray(L0_attn_mask, dtype=np.int32))
    gw = np.asarray(L0_ffn_gate_w, dtype=np.int32)
    uw = np.asarray(L0_ffn_up_w, dtype=np.int32)
    dw = np.asarray(L0_ffn_down_w, dtype=np.int32)

    wt = np.ascontiguousarray(W.reshape(32, 128, 32, 128).transpose(0, 3, 2, 1))
    gwt = np.ascontiguousarray(gw.reshape(NI, 128, 32, 128).transpose(0, 3, 2, 1))
    uwt = np.ascontiguousarray(uw.reshape(NI, 128, 32, 128).transpose(0, 3, 2, 1))
    dwt = np.ascontiguousarray(dw.T).reshape(NI, 128, D)
    scales = np.stack([np.asarray(L0_ffn_gate_s, np.float32).reshape(()),
                       np.asarray(L0_ffn_up_s, np.float32).reshape(()),
                       np.asarray(L0_ffn_down_s, np.float32).reshape(())]).reshape(1, 3)
    S_host, P_host = _host_consts()

    nc = build_nc()
    in_maps = []
    for b in range(8):
        in_maps.append({
            "xT": np.ascontiguousarray(x[b].T),
            "wt": wt, "mask": mask, "gwt": gwt, "uwt": uwt, "dwt": dwt,
            "scales": scales, "Ssel": S_host, "Psel": P_host,
        })
    res = run_bass_kernel_spmd(nc, in_maps, core_ids=list(range(8)))
    out = np.stack([np.ascontiguousarray(np.asarray(res.results[b]["outT"]).T)
                    for b in range(8)], axis=0)
    return out.astype(np.float32)
